# version 02
# baseline (speedup 1.0000x reference)
import numpy as np
import ml_dtypes
import concourse.bass as bass
import concourse.mybir as mybir
from concourse.bass_utils import run_bass_kernel_spmd

F32 = mybir.dt.float32
BF16 = mybir.dt.bfloat16
I32 = mybir.dt.int32
AF = mybir.ActivationFunctionType
ALU = mybir.AluOpType
bfdt = ml_dtypes.bfloat16

B, S, E = 4, 4096, 2048
H, D = 16, 128
TPC = 2048
I0, I1 = 5504, 8192
ROT = 64
NCORE = 8
RSQRT_D = 1.0 / float(np.sqrt(128.0))


class Sem:

    def __init__(self, prog, name):
        self.h = prog.ctx.enter_context(prog.nc.semaphore(name))
        self.n = 0

    def inc(self, instr, k=1):
        instr.then_inc(self.h, k)
        self.n += k
        return instr


class Prog:
    def __init__(self, name="prog"):
        self.nc = bass.Bass("TRN2", target_bir_lowering=False,
                            detect_race_conditions=False)
        from contextlib import ExitStack
        self.ctx = ExitStack()
        self.name = name

    def sem(self, name):
        return Sem(self, name)

    def sb(self, name, shape, dt):
        return self.ctx.enter_context(self.nc.sbuf_tensor(name, list(shape), dt))

    def ps(self, name, shape, dt=F32):
        return self.ctx.enter_context(self.nc.psum_tensor(name, list(shape), dt))

    def block(self):
        return self.ctx.enter_context(self.nc.Block())

    def close(self):
        self.ctx.close()
        return self.nc


def _run(nc, in_maps, **kw):
    return run_bass_kernel_spmd(nc, in_maps, core_ids=list(range(NCORE)), **kw)


def build_prep():
    p = Prog("prep")
    nc = p.nc
    i_wqkv = nc.dram_tensor("p_wqkv", [6144, 256], F32, kind="ExternalInput")
    i_wout = nc.dram_tensor("p_wout", [2048, 256], F32, kind="ExternalInput")
    i_l1w = nc.dram_tensor("p_l1w", [4 * 2048, 256], F32, kind="ExternalInput")
    i_g = nc.dram_tensor("p_gidx", [5504, 256], I32, kind="ExternalInput")
    i_u = nc.dram_tensor("p_uidx", [5504, 256], I32, kind="ExternalInput")
    i_d = nc.dram_tensor("p_didx", [2048, 688], I32, kind="ExternalInput")
    i_l1g = nc.dram_tensor("p_l1g", [8192, 256], I32, kind="ExternalInput")
    i_l1u = nc.dram_tensor("p_l1u", [8192, 256], I32, kind="ExternalInput")
    i_l1d = nc.dram_tensor("p_l1d", [2048, 1024], I32, kind="ExternalInput")
    i_lut = nc.dram_tensor("p_lut", [128, 16], F32, kind="ExternalInput")

    o_wqkv = nc.dram_tensor("o_wqkv", [6144, 256], BF16, kind="ExternalOutput")
    o_wout = nc.dram_tensor("o_wout", [2048, 256], BF16, kind="ExternalOutput")
    o_l1w = nc.dram_tensor("o_l1w", [4 * 2048, 256], BF16, kind="ExternalOutput")
    o_g = nc.dram_tensor("o_g", [5504, 256], BF16, kind="ExternalOutput")
    o_u = nc.dram_tensor("o_u", [5504, 256], BF16, kind="ExternalOutput")
    o_d = nc.dram_tensor("o_d", [2048, 688], BF16, kind="ExternalOutput")
    o_l1g = nc.dram_tensor("o_l1g", [8192, 256], BF16, kind="ExternalOutput")
    o_l1u = nc.dram_tensor("o_l1u", [8192, 256], BF16, kind="ExternalOutput")
    o_l1d = nc.dram_tensor("o_l1d", [2048, 1024], BF16, kind="ExternalOutput")
    o_dT = nc.dram_tensor("o_dT", [688, 2048], BF16, kind="ExternalOutput")
    n_d = nc.dram_tensor("n_d", [2048, 688], BF16, kind="Internal")
    n_l1d = nc.dram_tensor("n_l1d", [2048, 1024], BF16, kind="Internal")
    o_l1dT = nc.dram_tensor("o_l1dT", [1024, 2048], BF16, kind="ExternalOutput")

    s_cast = p.sb("s_cast", [128, 4096], BF16)
    s_lut = p.sb("s_lut", [128, 16], F32)
    s_idx = p.sb("s_idx", [128, 2752 * 2], I32)
    s_acc = p.sb("s_acc", [128, 2752 * 2], F32)
    s_trm = p.sb("s_trm", [128, 2752 * 2], F32)
    s_obf = p.sb("s_obf", [128, 2752 * 2], BF16)
    s_id = p.sb("s_id", [128, 256], BF16)
    s_dt = p.sb("s_dt", [128, 6 * 2048], BF16)
    s_d1 = p.sb("s_d1", [128, 2 * 1024], BF16)
    s_d1t = p.sb("s_d1t", [128, 8 * 2048], BF16)
    p_tr = p.ps("p_tr", [128, 2048], BF16)

    cast_jobs = [(i_wqkv, o_wqkv, 6144, 256), (i_wout, o_wout, 2048, 256),
                 (i_l1w, o_l1w, 8192, 256), (i_l1g, o_l1g, 8192, 256),
                 (i_l1u, o_l1u, 8192, 256), (i_l1d, o_l1d, 2048, 1024)]
    deq_jobs = [(i_g, o_g, 43, 256, 4), (i_u, o_u, 43, 256, 4), (i_d, o_d, 16, 688, 4)]

    lsems = [p.sem(f"lsem{i}") for i in range(4)]
    csems = [p.sem(f"csem{i}") for i in range(4)]
    qsems = [p.sem(f"qsem{i}") for i in range(2)]
    wsems = [p.sem(f"wsem{i}") for i in range(2)]
    lutsem = p.sem("lutsem")
    vsem = p.sem("vsem")
    idsem = p.sem("idsem")
    trs = p.sem("trs")
    tev = p.sem("tev")
    d1ld = [p.sem(f"d1ld{i}") for i in range(2)]
    dts = p.sem("dts")
    giota = p.sem("giota")
    ncp = p.sem("ncp")
    vsub = p.sem("vsub")

    groups = []
    for (src, dst, nch, cols, grp) in deq_jobs:
        for c0 in range(0, nch, grp):
            groups.append((src, dst, c0, min(grp, nch - c0), cols))

    block = p.block()

    cast_iters = [(src, dst, r0, cols) for (src, dst, rows, cols) in cast_jobs
                  for r0 in range(0, rows, 128)]
    sub_srcs = (i_l1g, i_l1u, i_l1d)
    sub_iters = [n for n, it in enumerate(cast_iters) if it[0] in sub_srcs]
    sub_rank = {n: i for i, n in enumerate(sub_iters)}

    @block.gpsimd
    def _(g):
        for n, (src, dst, r0, cols) in enumerate(cast_iters):
            buf = (n % 4) * 1024
            if n >= 4:
                g.wait_ge(csems[n % 4].h, 16 * (n // 4))
            lsems[n % 4].inc(g.dma_start(s_cast[:, buf:buf + cols], src[r0:r0 + 128, :]), 16)

    @block.sync
    def _(sy):
        lutsem.inc(sy.dma_start(s_lut[:, :], i_lut[:, :]), 16)
        for gi, (src, dst, c0, g_n, cols) in enumerate(groups):
            buf = (gi % 2) * 2752
            if gi >= 2:
                sy.wait_ge(vsem.h, gi - 1)
            for j in range(g_n):
                qsems[gi % 2].inc(sy.dma_start(s_idx[:, buf + j * cols: buf + (j + 1) * cols],
                                               src[(c0 + j) * 128:(c0 + j + 1) * 128, :]), 16)

    @block.vector
    def _(v):
        lcnt2 = [0, 0, 0, 0]
        for n, (srct, dst, r0, cols) in enumerate(cast_iters):
            lcnt2[n % 4] += 16
            if srct in sub_srcs:
                buf = (n % 4) * 1024
                v.wait_ge(lsems[n % 4].h, lcnt2[n % 4])
                vsub.inc(v.tensor_scalar(s_cast[:, buf:buf + cols], s_cast[:, buf:buf + cols],
                                         128.0, None, ALU.subtract), 1)
        v.wait_ge(lutsem.h, 16)
        qcnt = [0, 0]
        wcnt = [0, 0]
        for gi, (src, dst, c0, g_n, cols) in enumerate(groups):
            w = g_n * cols
            buf = (gi % 2) * 2752
            qcnt[gi % 2] += 16 * g_n
            v.wait_ge(qsems[gi % 2].h, qcnt[gi % 2])
            idx = s_idx[:, buf:buf + w]
            ab = (gi % 2) * 2752
            acc = s_acc[:, ab:ab + w]
            trm = s_trm[:, ab:ab + w]
            v.tensor_scalar(acc, idx, 0.0, s_lut[:, 0:1], ALU.is_equal, ALU.mult)
            for j in range(1, 16):
                v.tensor_scalar(trm, idx, float(j), s_lut[:, j:j + 1],
                                ALU.is_equal, ALU.mult)
                v.tensor_add(acc, acc, trm)
            if gi >= 2:
                wcnt[gi % 2] += 16 * groups[gi - 2][3]
                v.wait_ge(wsems[gi % 2].h, wcnt[gi % 2])
            vsem.inc(v.tensor_copy(s_obf[:, buf:buf + w], acc), 1)

    @block.scalar
    def _(sc):
        lcnt = [0, 0, 0, 0]
        for n, (src, dst, r0, cols) in enumerate(cast_iters):
            buf = (n % 4) * 1024
            lcnt[n % 4] += 16
            sc.wait_ge(lsems[n % 4].h, lcnt[n % 4])
            if n in sub_rank:
                sc.wait_ge(vsub.h, sub_rank[n] + 1)
            csems[n % 4].inc(sc.dma_start(dst[r0:r0 + 128, :], s_cast[:, buf:buf + cols]), 16)
            if dst is o_l1d:
                ncp.inc(sc.dma_start(n_l1d[r0:r0 + 128, :], s_cast[:, buf:buf + cols]), 16)
        for gi, (src, dst, c0, g_n, cols) in enumerate(groups):
            buf = (gi % 2) * 2752
            sc.wait_ge(vsem.h, gi + 1)
            for j in range(g_n):
                wsems[gi % 2].inc(
                    sc.dma_start(dst[(c0 + j) * 128:(c0 + j + 1) * 128, :],
                                 s_obf[:, buf + j * cols:buf + (j + 1) * cols]), 16)
                if dst is o_d:
                    ncp.inc(
                        sc.dma_start(n_d[(c0 + j) * 128:(c0 + j + 1) * 128, :],
                                     s_obf[:, buf + j * cols:buf + (j + 1) * cols]), 16)
        for s in wsems + csems:
            sc.wait_ge(s.h, s.n)

    tr_jobs = []
    for hc in range(16):
        tr_jobs.append(("d0", hc))
    for hc in range(16):
        tr_jobs.append(("d1", hc))
    d0_pieces = [128, 128, 128, 128, 128, 48]
    PIECES = {"d0": d0_pieces, "d1": [128] * 8}
    piece_cum = []
    tot = 0
    for (kind, hc) in tr_jobs:
        tot += len(PIECES[kind])
        piece_cum.append(tot)
    NPIECE = tot

    @block.gpsimd
    def _(g):
        g.iota(s_id[:, 128:256], [[1, 128]], channel_multiplier=0,
               allow_small_or_imprecise_dtypes=True)
        giota.inc(g.iota(s_id[:, 0:128], [[0, 128]], channel_multiplier=1,
                         allow_small_or_imprecise_dtypes=True), 1)
        g.wait_ge(wsems[0].h, wsems[0].n)
        g.wait_ge(wsems[1].h, wsems[1].n)
        g.wait_ge(csems[0].h, csems[0].n)
        g.wait_ge(csems[1].h, csems[1].n)
        g.wait_ge(csems[2].h, csems[2].n)
        g.wait_ge(csems[3].h, csems[3].n)
        g.wait_ge(ncp.h, ncp.n)
        cnt = [0, 0]
        for n, (kind, hc) in enumerate(tr_jobs):
            if n >= 2:
                g.wait_ge(trs.h, piece_cum[n - 2])
            if kind == "d0":
                d1ld[n % 2].inc(g.dma_start(s_d1[:, (n % 2) * 1024:(n % 2) * 1024 + 688],
                                            n_d[128 * hc:128 * (hc + 1), :]), 16)
            else:
                d1ld[n % 2].inc(g.dma_start(s_d1[:, (n % 2) * 1024:(n % 2) * 1024 + 1024],
                                            n_l1d[128 * hc:128 * (hc + 1), :]), 16)
            cnt[n % 2] += 16
        g.wait_ge(tev.h, NPIECE)
        for pi, pw in enumerate(d0_pieces):
            dts.inc(g.dma_start(o_dT[128 * pi:128 * pi + pw, :],
                                s_dt[0:pw, 2048 * pi:2048 * pi + 2048]), 16)
        for pi in range(8):
            dts.inc(g.dma_start(o_l1dT[128 * pi:128 * (pi + 1), :],
                                s_d1t[:, 2048 * pi:2048 * (pi + 1)]), 16)
        g.wait_ge(dts.h, dts.n)

    @block.vector
    def _(v):
        v.wait_ge(giota.h, 1)
        idsem.inc(v.tensor_tensor(s_id[:, 0:128], s_id[:, 0:128], s_id[:, 128:256],
                                  ALU.is_equal), 1)
        pidx2 = 0
        for n, (kind, hc) in enumerate(tr_jobs):
            for pi, pw in enumerate(PIECES[kind]):
                v.wait_ge(trs.h, pidx2 + 1)
                if kind == "d0":
                    dst = s_dt[0:pw, 2048 * pi + 128 * hc:2048 * pi + 128 * (hc + 1)]
                else:
                    dst = s_d1t[0:pw, 2048 * pi + 128 * hc:2048 * pi + 128 * (hc + 1)]
                tev.inc(v.tensor_copy(dst, p_tr[0:pw, (pidx2 % 2) * 1024:(pidx2 % 2) * 1024 + 128]), 1)
                pidx2 += 1

    @block.tensor
    def _(t):
        t.wait_ge(idsem.h, 1)
        pidx = 0
        cnt = [0, 0]
        for n, (kind, hc) in enumerate(tr_jobs):
            cnt[n % 2] += 16
            t.wait_ge(d1ld[n % 2].h, cnt[n % 2])
            for pi, pw in enumerate(PIECES[kind]):
                if pidx >= 2:
                    t.wait_ge(tev.h, pidx - 1)
                trs.inc(t.transpose(p_tr[0:pw, (pidx % 2) * 1024:(pidx % 2) * 1024 + 128],
                                    s_d1[:, (n % 2) * 1024 + 128 * pi:(n % 2) * 1024 + 128 * pi + pw],
                                    s_id[:, 0:128]), 1)
                pidx += 1

    p.close()
    return p.nc


def host_prep(inputs):
    w_qkv = inputs["L0_attn_w_qkv"]
    w_out = inputs["L0_attn_w_out"]
    l1w = np.concatenate([inputs["L1_attn_q_w"], inputs["L1_attn_k_w"],
                          inputs["L1_attn_v_w"], inputs["L1_attn_o_w"]], axis=0)
    lut = np.broadcast_to(inputs["L0_ffn_lut"].reshape(1, 16), (128, 16)).copy()
    in_maps = []
    for r in range(NCORE):
        c2 = slice(256 * r, 256 * (r + 1))
        in_maps.append(dict(
            p_wqkv=np.ascontiguousarray(w_qkv[:, c2]),
            p_wout=np.ascontiguousarray(w_out[:, c2]),
            p_l1w=np.ascontiguousarray(l1w[:, c2]),
            p_gidx=np.ascontiguousarray(inputs["L0_ffn_gate_idx"][:, c2]),
            p_uidx=np.ascontiguousarray(inputs["L0_ffn_up_idx"][:, c2]),
            p_didx=np.ascontiguousarray(inputs["L0_ffn_down_idx"][:, 688 * r:688 * (r + 1)]),
            p_l1g=np.ascontiguousarray(inputs["L1_ffn_gate_w"][:, c2]),
            p_l1u=np.ascontiguousarray(inputs["L1_ffn_up_w"][:, c2]),
            p_l1d=np.ascontiguousarray(inputs["L1_ffn_down_w"][:, 1024 * r:1024 * (r + 1)]),
            p_lut=lut,
        ))
    return in_maps


def assemble_prep(results):
    cat = lambda k, ax: np.concatenate([results[r][k] for r in range(NCORE)], axis=ax)
    w = {}
    w["qkv"] = cat("o_wqkv", 1)
    w["wout"] = cat("o_wout", 1)
    l1w = cat("o_l1w", 1)
    w["l1q"], w["l1k"], w["l1v"], w["l1o"] = (l1w[2048 * i:2048 * (i + 1)] for i in range(4))
    w["g0"] = cat("o_g", 1)
    w["u0"] = cat("o_u", 1)
    w["d0"] = cat("o_d", 1)
    w["g1"] = cat("o_l1g", 1)
    w["u1"] = cat("o_l1u", 1)
    w["d1"] = cat("o_l1d", 1)
    w["d0T"] = np.concatenate([results[r]["o_dT"] for r in range(NCORE)], axis=0)
    w["d1T"] = np.concatenate([results[r]["o_l1dT"] for r in range(NCORE)], axis=0)
    return w


def build_l0a():
    p = Prog("l0a")
    nc = p.nc
    i_x = nc.dram_tensor("x_T", [E, TPC], F32, kind="ExternalInput")
    i_w = nc.dram_tensor("wqkv_bf", [6144, E], BF16, kind="ExternalInput")
    i_cos = nc.dram_tensor("cosT", [ROT, TPC], F32, kind="ExternalInput")
    i_sin = nc.dram_tensor("sinT", [ROT, TPC], F32, kind="ExternalInput")
    i_rt = nc.dram_tensor("R64T", [ROT, ROT], BF16, kind="ExternalInput")

    o_q = nc.dram_tensor("o_q", [H, D, TPC], BF16, kind="ExternalOutput")
    o_k = nc.dram_tensor("o_k", [H, D, TPC], BF16, kind="ExternalOutput")
    o_vr = nc.dram_tensor("o_vr", [4, 16, 128, 512], BF16, kind="ExternalOutput")
    o_xn = nc.dram_tensor("o_xn", [E, TPC], BF16, kind="ExternalOutput")

    NT, TT, EC = 4, 512, 16

    s_x = p.sb("s_x", [128, EC * TT], F32)
    s_xn = p.sb("s_xn", [128, EC * TPC], BF16)
    s_sq = p.sb("s_sq", [128, 2 * TT], BF16)
    s_rows = p.sb("s_rows", [1, 6 * TT], F32)
    s_rbf = p.sb("s_rbf", [1, 2 * TT], BF16)
    s_cst = p.sb("s_cst", [1, 2], F32)
    s_ones = p.sb("s_ones", [128, 1], BF16)
    s_onr = p.sb("s_onr", [1, 128], BF16)
    s_wg = p.sb("s_wg", [128, EC * TT], BF16)
    s_wv = p.sb("s_wv", [128, EC * TT], BF16)
    s_qh = p.sb("s_qh", [128, 4 * 2048], BF16)
    s_rot = p.sb("s_rot", [64, 2 * 2048], BF16)
    s_t1 = p.sb("s_t1", [64, 2 * TT], F32)
    s_t2 = p.sb("s_t2", [64, 2 * TT], F32)
    s_cos = p.sb("s_cos", [64, TPC], F32)
    s_sin = p.sb("s_sin", [64, TPC], F32)
    s_rt = p.sb("s_rt", [64, 64], BF16)
    s_vout = p.sb("s_vout", [128, 2 * TT], BF16)
    s_tmp = p.sb("s_tmp", [128, 2 * TT], F32)

    p_mm = p.ps("p_mm", [128, 2048], F32)
    p_bc = p.ps("p_bc", [128, 1024], F32)
    p_rp = p.ps("p_rp", [64, 1024], F32)

    xld = [p.sem(f"xld{i}") for i in range(2)]
    wgs = p.sem("wgs")
    wvs = p.sem("wvs")
    misc = p.sem("misc")
    vxb = p.sem("vxb")
    sqs = p.sem("sqs")
    pesq = p.sem("pesq")
    rowv = p.sem("rowv")
    rowr = p.sem("rowr")
    rowb = p.sem("rowb")
    bcm = p.sem("bcm")
    vdone = p.sem("vdone")
    onesd = p.sem("onesd")
    qkmm = p.sem("qkmm")
    ehd = p.sem("ehd")
    rpm = p.sem("rpm")
    vrp = p.sem("vrp")
    hdma = [p.sem(f"hdma{i}") for i in range(2)]
    xnst = p.sem("xnst")
    vmm = p.sem("vmm")
    vev = p.sem("vev")
    vstq = [p.sem(f"vstq{i}") for i in range(2)]

    fgroups = [(g * 1536, 'q', g) for g in range(4)] + [(g * 1536 + 1024, 'k', g) for g in range(4)]
    NIDX = 32

    block = p.block()

    @block.sync
    def _(sy):
        misc.inc(sy.dma_start(s_cos[:, :], i_cos[:, :]), 16)
        misc.inc(sy.dma_start(s_sin[:, :], i_sin[:, :]), 16)
        misc.inc(sy.dma_start(s_rt[:, :], i_rt[:, :]), 16)
        for ec in range(EC):
            xld[0].inc(sy.dma_start(s_x[:, TT * ec:TT * (ec + 1)],
                                    i_x[128 * ec:128 * (ec + 1), 0:TT]), 16)
        for ec in range(EC):
            wgs.inc(sy.dma_start_transpose(s_wg[:, TT * ec:TT * (ec + 1)],
                                           i_w[0:512, 128 * ec:128 * (ec + 1)]), 16)
        for tt in range(1, NT):
            sy.wait_ge(vdone.h, tt)
            for ec in range(EC):
                xld[tt % 2].inc(sy.dma_start(s_x[:, TT * ec:TT * (ec + 1)],
                                             i_x[128 * ec:128 * (ec + 1), TT * tt:TT * (tt + 1)]), 16)
        for gi in range(1, 8):
            fstart = fgroups[gi][0]
            sy.wait_ge(qkmm.h, 4 * gi)
            for ec in range(EC):
                wgs.inc(sy.dma_start_transpose(s_wg[:, TT * ec:TT * (ec + 1)],
                                               i_w[fstart:fstart + 512, 128 * ec:128 * (ec + 1)]), 16)
        for g in range(4):
            if g == 0:
                sy.wait_ge(qkmm.h, NIDX)
            else:
                sy.wait_ge(vmm.h, 16 * g)
            for ec in range(EC):
                wvs.inc(sy.dma_start_transpose(s_wv[:, TT * ec:TT * (ec + 1)],
                                               i_w[1536 * g + 512:1536 * g + 1024,
                                                   128 * ec:128 * (ec + 1)]), 16)

    @block.vector
    def _(v):
        v.memset(s_ones[:, :], 1.0)
        v.memset(s_cst[:, 0:1], 1e-5)
        v.memset(s_cst[:, 1:2], -0.5)
        onesd.inc(v.memset(s_onr[:, :], 1.0), 1)
        for tt in range(NT):
            v.wait_ge(xld[tt % 2].h, 16 * EC * (tt // 2 + 1))
            for ec in range(EC):
                vxb.inc(v.tensor_copy(s_xn[:, TPC * ec + TT * tt: TPC * ec + TT * (tt + 1)],
                                      s_x[:, TT * ec:TT * (ec + 1)]), 1)
            v.wait_ge(pesq.h, 16 * (tt + 1))
            v.tensor_scalar(s_rows[:, 0:TT], p_bc[0:1, 0:TT], 1.0 / E, None, ALU.mult)
            v.tensor_scalar(s_rows[:, TT:2 * TT], p_bc[0:1, 512:512 + TT], 1.0 / E, None, ALU.mult)
            v.tensor_mul(s_rows[:, 5 * TT:6 * TT], s_rows[:, 0:TT], s_rows[:, 0:TT])
            rowv.inc(v.tensor_sub(s_rows[:, 2 * TT:3 * TT], s_rows[:, TT:2 * TT],
                                  s_rows[:, 5 * TT:6 * TT]), 1)
            v.wait_ge(rowr.h, tt + 1)
            v.tensor_copy(s_rbf[:, 0:TT], s_rows[:, 0:TT])
            rowb.inc(v.tensor_copy(s_rbf[:, TT:2 * TT], s_rows[:, 4 * TT:5 * TT]), 1)
            v.wait_ge(bcm.h, 2 * (tt + 1))
            for ec in range(EC):
                bsel = ec % 2
                v.tensor_sub(s_tmp[:, bsel * TT:(bsel + 1) * TT],
                             s_x[:, TT * ec:TT * (ec + 1)], p_bc[:, 0:TT])
                last = v.tensor_mul(s_xn[:, TPC * ec + TT * tt: TPC * ec + TT * (tt + 1)],
                                    s_tmp[:, bsel * TT:(bsel + 1) * TT], p_bc[:, 512:512 + TT])
            vdone.inc(last, 1)
        for idx in range(NIDX):
            bsel, rsel = idx % 4, idx % 2
            for tt in range(NT):
                v.wait_ge(rpm.h, 4 * idx + tt + 1)
                if tt == 0 and idx >= 2:
                    v.wait_ge(hdma[idx % 2].h, 32 * (idx // 2))
                ts2 = tt % 2
                v.tensor_mul(s_t1[:, ts2 * TT:(ts2 + 1) * TT],
                             s_qh[0:64, bsel * 2048 + TT * tt: bsel * 2048 + TT * (tt + 1)],
                             s_cos[:, TT * tt:TT * (tt + 1)])
                v.tensor_mul(s_t2[:, ts2 * TT:(ts2 + 1) * TT],
                             p_rp[0:64, ts2 * 512:ts2 * 512 + TT],
                             s_sin[:, TT * tt:TT * (tt + 1)])
                vrp.inc(v.tensor_add(s_rot[0:64, rsel * 2048 + TT * tt: rsel * 2048 + TT * (tt + 1)],
                                     s_t1[:, ts2 * TT:(ts2 + 1) * TT],
                                     s_t2[:, ts2 * TT:(ts2 + 1) * TT]), 1)

    @block.scalar
    def _(sc):
        for tt in range(NT):
            for ec in range(EC):
                sc.wait_ge(vxb.h, 16 * tt + ec + 1)
                if 16 * tt + ec >= 2:
                    sc.wait_ge(pesq.h, 16 * tt + ec - 1)
                sqs.inc(sc.activation(s_sq[:, (ec % 2) * TT:(ec % 2 + 1) * TT],
                                      s_xn[:, TPC * ec + TT * tt: TPC * ec + TT * (tt + 1)],
                                      AF.Square), 1)
            sc.wait_ge(rowv.h, tt + 1)
            sc.wait_ge(onesd.h, 1)
            sc.activation(s_rows[:, 3 * TT:4 * TT], s_rows[:, 2 * TT:3 * TT], AF.Ln,
                          bias=s_cst[0:1, 0:1])
            rowr.inc(sc.activation(s_rows[:, 4 * TT:5 * TT], s_rows[:, 3 * TT:4 * TT],
                                   AF.Exp, scale=s_cst[0:1, 1:2]), 1)
        for idx in range(NIDX):
            bsel = idx % 4
            sc.wait_ge(qkmm.h, idx + 1)
            if idx >= 4:
                sc.wait_ge(hdma[idx % 2].h, 32 * (idx // 2 - 1))
            for tt in range(NT):
                last = sc.copy(s_qh[:, bsel * 2048 + TT * tt: bsel * 2048 + TT * (tt + 1)],
                               p_mm[:, 512 * tt:512 * (tt + 1)])
            ehd.inc(last, 1)
        for c in range(64):
            sc.wait_ge(vmm.h, c + 1)
            if c >= 2:
                sc.wait_ge(vstq[c % 2].h, 16 * (c // 2))
            vev.inc(sc.copy(s_vout[:, (c % 2) * TT:(c % 2 + 1) * TT],
                            p_mm[:, (c % 2) * 512:(c % 2) * 512 + 512]), 1)

    @block.tensor
    def _(t):
        t.wait_ge(misc.h, 48)
        t.wait_ge(onesd.h, 1)
        for tt in range(NT):
            if tt >= 1:
                t.wait_ge(vdone.h, tt)
            for ec in range(EC):
                t.wait_ge(vxb.h, 16 * tt + ec + 1)
                t.matmul(p_bc[0:1, 0:512], s_ones[:, :],
                         s_xn[:, TPC * ec + TT * tt: TPC * ec + TT * (tt + 1)],
                         start=(ec == 0), stop=(ec == EC - 1))
                t.wait_ge(sqs.h, 16 * tt + ec + 1)
                mm2 = t.matmul(p_bc[0:1, 512:1024], s_ones[:, :],
                               s_sq[:, (ec % 2) * TT:(ec % 2 + 1) * TT],
                               start=(ec == 0), stop=(ec == EC - 1))
                pesq.inc(mm2, 1)
            t.wait_ge(rowb.h, tt + 1)
            bcm.inc(t.matmul(p_bc[:, 0:512], s_onr[:, :], s_rbf[:, 0:TT],
                             start=True, stop=True), 1)
            bcm.inc(t.matmul(p_bc[:, 512:1024], s_onr[:, :], s_rbf[:, TT:2 * TT],
                             start=True, stop=True), 1)
        def rope_mms(j):
            for tt in range(NT):
                if 4 * j + tt >= 2:
                    t.wait_ge(vrp.h, 4 * j + tt - 1)
                rpm.inc(t.matmul(p_rp[0:64, (tt % 2) * 512:(tt % 2) * 512 + 512],
                                 s_rt[:, :],
                                 s_qh[0:64, (j % 4) * 2048 + TT * tt: (j % 4) * 2048 + TT * (tt + 1)],
                                 start=True, stop=True), 1)

        for idx in range(NIDX):
            gi, fc = idx // 4, idx % 4
            if idx == 0:
                t.wait_ge(vdone.h, NT)
            t.wait_ge(wgs.h, 16 * 16 * (gi + 1))
            if idx >= 1:
                t.wait_ge(ehd.h, idx)
            for ec in range(EC):
                for tt in range(NT):
                    mm = t.matmul(p_mm[:, 512 * tt:512 * (tt + 1)],
                                  s_wg[:, TT * ec + 128 * fc: TT * ec + 128 * (fc + 1)],
                                  s_xn[:, TPC * ec + TT * tt: TPC * ec + TT * (tt + 1)],
                                  start=(ec == 0), stop=(ec == EC - 1))
            qkmm.inc(mm, 1)
            if idx >= 2:
                rope_mms(idx - 2)
        rope_mms(NIDX - 2)
        t.wait_ge(ehd.h, NIDX)
        for g in range(4):
            t.wait_ge(wvs.h, 16 * 16 * (g + 1))
            for tc in range(16):
                c = 16 * g + tc
                if c >= 2:
                    t.wait_ge(vev.h, c - 1)
                for ec in range(EC):
                    mm = t.matmul(p_mm[:, (c % 2) * 512:(c % 2) * 512 + 512],
                                  s_xn[:, TPC * ec + 128 * tc: TPC * ec + 128 * (tc + 1)],
                                  s_wv[:, TT * ec:TT * (ec + 1)],
                                  start=(ec == 0), stop=(ec == EC - 1))
                vmm.inc(mm, 1)
                if c == 0:
                    rope_mms(NIDX - 1)

    @block.gpsimd
    def _(g):
        g.wait_ge(vdone.h, NT)
        for ec in range(EC):
            xnst.inc(g.dma_start(o_xn[128 * ec:128 * (ec + 1), :],
                                 s_xn[:, TPC * ec:TPC * (ec + 1)]), 16)
        for idx in range(NIDX):
            fstart, kind, grp = fgroups[idx // 4]
            fc = idx % 4
            h = 4 * grp + fc
            dst = o_q if kind == 'q' else o_k
            bsel, rsel = idx % 4, idx % 2
            g.wait_ge(vrp.h, 4 * (idx + 1))
            hdma[idx % 2].inc(g.dma_start(dst[h, 0:64, :],
                                          s_rot[0:64, rsel * 2048:(rsel + 1) * 2048]), 16)
            hdma[idx % 2].inc(g.dma_start(dst[h, 64:128, :],
                                          s_qh[64:128, bsel * 2048:(bsel + 1) * 2048]), 16)
        for c in range(64):
            gg, tc = c // 16, c % 16
            g.wait_ge(vev.h, c + 1)
            vstq[c % 2].inc(g.dma_start(o_vr[gg, tc, :, :],
                                        s_vout[:, (c % 2) * TT:(c % 2 + 1) * TT]), 16)
        g.wait_ge(xnst.h, 16 * EC)
        g.wait_ge(hdma[0].h, hdma[0].n)
        g.wait_ge(hdma[1].h, hdma[1].n)
        g.wait_ge(vstq[0].h, vstq[0].n)
        g.wait_ge(vstq[1].h, vstq[1].n)

    p.close()
    return p.nc


def host_l0a(inputs, w):
    x = np.asarray(inputs["L0_attn_hidden_states"]).reshape(B * S, E)
    inv_freq = 1.0 / 10000.0 ** (np.arange(0, ROT, 2, dtype=np.float32) / ROT)
    in_maps = []
    for c in range(NCORE):
        toks = slice(c * TPC, (c + 1) * TPC)
        pos = np.arange((c % 2) * TPC, (c % 2 + 1) * TPC, dtype=np.float32)
        freqs = pos[:, None] * inv_freq[None, :]
        cosT = np.repeat(np.cos(freqs), 2, axis=1).T.astype(np.float32)
        sinT = np.repeat(np.sin(freqs), 2, axis=1).T.astype(np.float32)
        rt = np.zeros((ROT, ROT), dtype=np.float32)
        for i in range(ROT // 2):
            rt[2 * i + 1, 2 * i] = -1.0
            rt[2 * i, 2 * i + 1] = 1.0
        in_maps.append(dict(
            x_T=np.ascontiguousarray(x[toks].T),
            wqkv_bf=w["qkv"],
            cosT=np.ascontiguousarray(cosT),
            sinT=np.ascontiguousarray(sinT),
            R64T=rt.astype(bfdt),
        ))
    return in_maps


def v_from_raw(o_vr):
    r = o_vr.reshape(4, 16, 128, 4, 128)
    r = r.transpose(0, 3, 1, 2, 4)
    return np.ascontiguousarray(r.reshape(16, 2048, 128))


def build_l0attn(NH=8, NJ=8):
    p = Prog("l0attn")
    nc = p.nc
    S = NJ * 512
    i_q = nc.dram_tensor("Q_in", [NH, D, S], BF16, kind="ExternalInput")
    i_k = nc.dram_tensor("K_in", [NH, D, S], BF16, kind="ExternalInput")
    i_v = nc.dram_tensor("V_in", [NH, S, D], BF16, kind="ExternalInput")
    i_m = nc.dram_tensor("tri", [128, 4 * 512], BF16, kind="ExternalInput")
    o_a = nc.dram_tensor("AO", [NH, D, S], BF16, kind="ExternalOutput")

    s_q = p.sb("s_q", [128, 2 * S], BF16)
    s_k = p.sb("s_k", [128, 2 * S], BF16)
    s_v = p.sb("s_v", [128, 2 * S], BF16)
    s_m = p.sb("s_m", [128, 4 * 512], BF16)
    s_at = p.sb("s_at", [128, 2 * 512], BF16)
    s_zr = p.sb("s_zr", [1, 512], F32)
    s_zb = p.sb("s_zb", [1, 512], BF16)
    s_zbc = p.sb("s_zbc", [128, 2 * 512], F32)
    s_ot = p.sb("s_ot", [128, 2 * 512], F32)
    s_ao = p.sb("s_ao", [128, 2 * 512], BF16)
    s_ones = p.sb("s_ones", [128, 1], BF16)
    s_onr = p.sb("s_onr", [1, 128], BF16)
    s_sc = p.sb("s_sc", [128, 1], F32)

    p_sc = p.ps("p_sc", [128, 1024], F32)
    p_out = p.ps("p_out", [128, 512], F32)
    p_z = p.ps("p_z", [1, 512], F32)
    p_bz = p.ps("p_bz", [128, 512], F32)

    ld = [p.sem(f"ld{i}") for i in range(2)]
    msk = p.sem("mld")
    onesd = p.sem("onesd")
    psem = p.sem("psem")
    esem = p.sem("esem")
    msem = p.sem("msem")
    avz = p.sem("avz")
    zrd = p.sem("zrd")
    zbc = p.sem("zbc")
    zcp = p.sem("zcp")
    oev = p.sem("oev")
    ost = [p.sem(f"ost{i}") for i in range(2)]

    units = [(h, j) for h in range(NH) for j in range(NJ)]
    nk_of = {(h, j): 4 * (j + 1) for (h, j) in units}
    chunk_list = []
    cg = dg = 0
    for (h, j) in units:
        for ck in range(nk_of[(h, j)]):
            diag = ck >= 4 * j
            chunk_list.append((h, j, ck, diag, cg, dg))
            cg += 1
            if diag:
                dg += 1
    NCH = cg
    cum_u = np.cumsum([nk_of[u] for u in units])
    CPH = NJ * (NJ + 1) * 2
    by_unit = {}
    for rec in chunk_list:
        by_unit.setdefault((rec[0], rec[1]), []).append(rec)

    block = p.block()

    @block.sync
    def _(sy):
        msk.inc(sy.dma_start(s_m[:, :], i_m[:, :]), 16)
        for h in range(NH):
            if h >= 2:
                sy.wait_ge(avz.h, CPH * (h - 1))
            b = (h % 2) * S
            for r in range(S // 1024):
                ld[h % 2].inc(sy.dma_start(s_q[:, b + 1024 * r:b + 1024 * (r + 1)],
                                           i_q[h, :, 1024 * r:1024 * (r + 1)]), 16)
                ld[h % 2].inc(sy.dma_start(s_k[:, b + 1024 * r:b + 1024 * (r + 1)],
                                           i_k[h, :, 1024 * r:1024 * (r + 1)]), 16)
                for kc in range(8):
                    ld[h % 2].inc(sy.dma_start(
                        s_v[:, b + 1024 * r + 128 * kc:b + 1024 * r + 128 * (kc + 1)],
                        i_v[h, 1024 * r + 128 * kc:1024 * r + 128 * (kc + 1), :]), 16)

    @block.tensor
    def _(t):
        t.wait_ge(msk.h, 16)
        t.wait_ge(onesd.h, 1)
        for u, (h, j) in enumerate(units):
            hb = (h % 2) * S
            if j == 0:
                t.wait_ge(ld[h % 2].h, 16 * 10 * (S // 1024) * (h // 2 + 1))
            nk = nk_of[(h, j)]
            for (_, _, ck, diag, cgi, dgi) in by_unit[(h, j)]:
                if cgi >= 2:
                    t.wait_ge(esem.h, cgi - 1)
                psem.inc(t.matmul(p_sc[:, (cgi % 2) * 512:(cgi % 2) * 512 + 512],
                                  s_k[:, hb + 128 * ck:hb + 128 * (ck + 1)],
                                  s_q[:, hb + 512 * j:hb + 512 * (j + 1)],
                                  start=True, stop=True), 1)
                t.wait_ge(esem.h, cgi + 1)
                if diag:
                    t.wait_ge(msem.h, dgi + 1)
                if ck == 0:
                    if u >= 1:
                        t.wait_ge(oev.h, u)
                        t.wait_ge(zrd.h, u)
                t.matmul(p_out[:, :],
                         s_v[:, hb + 128 * ck:hb + 128 * (ck + 1)],
                         s_at[:, (cgi % 2) * 512:(cgi % 2) * 512 + 512],
                         start=(ck == 0), stop=(ck == nk - 1))
                avz.inc(t.matmul(p_z[:, :], s_ones[:, :],
                                 s_at[:, (cgi % 2) * 512:(cgi % 2) * 512 + 512],
                                 start=(ck == 0), stop=(ck == nk - 1)), 1)
            t.wait_ge(zrd.h, u + 1)
            zbc.inc(t.matmul(p_bz[:, :], s_onr[:, :], s_zb[:, :], start=True, stop=True), 1)

    @block.scalar
    def _(sc):
        sc.wait_ge(onesd.h, 1)
        for u, (h, j) in enumerate(units):
            for (_, _, ck, diag, cgi, dgi) in by_unit[(h, j)]:
                sc.wait_ge(psem.h, cgi + 1)
                if cgi >= 2:
                    sc.wait_ge(avz.h, cgi - 1)
                esem.inc(sc.activation(s_at[:, (cgi % 2) * 512:(cgi % 2) * 512 + 512],
                                       p_sc[:, (cgi % 2) * 512:(cgi % 2) * 512 + 512],
                                       AF.Exp, scale=s_sc[:, :]), 1)
            sc.wait_ge(zbc.h, u + 1)
            zcp.inc(sc.copy(s_zbc[:, (u % 2) * 512:(u % 2) * 512 + 512], p_bz[:, :]), 1)

    @block.vector
    def _(v):
        v.memset(s_ones[:, :], 1.0)
        v.memset(s_sc[:, :], RSQRT_D)
        onesd.inc(v.memset(s_onr[:, :], 1.0), 1)
        v.wait_ge(msk.h, 16)
        for u, (h, j) in enumerate(units):
            for (_, _, ck, diag, cgi, dgi) in by_unit[(h, j)]:
                if not diag:
                    continue
                u2 = ck - 4 * j
                v.wait_ge(esem.h, cgi + 1)
                msem.inc(v.tensor_mul(s_at[:, (cgi % 2) * 512:(cgi % 2) * 512 + 512],
                                      s_at[:, (cgi % 2) * 512:(cgi % 2) * 512 + 512],
                                      s_m[:, 512 * u2:512 * (u2 + 1)]), 1)
            v.wait_ge(avz.h, int(cum_u[u]))
            v.tensor_copy(s_zr[:, :], p_z[0:1, :])
            v.reciprocal(s_zr[:, :], s_zr[:, :])
            zrd.inc(v.tensor_copy(s_zb[:, :], s_zr[:, :]), 1)
            v.tensor_copy(s_ot[:, (u % 2) * 512:(u % 2) * 512 + 512], p_out[:, :])
            v.wait_ge(zcp.h, u + 1)
            if u >= 2:
                v.wait_ge(ost[u % 2].h, 16 * (u // 2))
            oev.inc(v.tensor_mul(s_ao[:, (u % 2) * 512:(u % 2) * 512 + 512],
                                 s_ot[:, (u % 2) * 512:(u % 2) * 512 + 512],
                                 s_zbc[:, (u % 2) * 512:(u % 2) * 512 + 512]), 1)

    @block.gpsimd
    def _(g):
        for u, (h, j) in enumerate(units):
            g.wait_ge(oev.h, u + 1)
            ost[u % 2].inc(g.dma_start(o_a[h, :, 512 * j:512 * (j + 1)],
                                       s_ao[:, (u % 2) * 512:(u % 2) * 512 + 512]), 16)
        g.wait_ge(ost[0].h, ost[0].n)
        g.wait_ge(ost[1].h, ost[1].n)

    p.close()
    return p.nc


def host_l0attn(l2, tri):
    in_maps = []
    for c in range(NCORE):
        b = c // 2
        hs = (c % 2) * 8
        q = np.concatenate([l2[2 * b]["o_q"][hs:hs + 8], l2[2 * b + 1]["o_q"][hs:hs + 8]], axis=2)
        k = np.concatenate([l2[2 * b]["o_k"][hs:hs + 8], l2[2 * b + 1]["o_k"][hs:hs + 8]], axis=2)
        v0 = v_from_raw(l2[2 * b]["o_vr"])[hs:hs + 8]
        v1 = v_from_raw(l2[2 * b + 1]["o_vr"])[hs:hs + 8]
        v = np.concatenate([v0, v1], axis=1)
        in_maps.append(dict(Q_in=np.ascontiguousarray(q), K_in=np.ascontiguousarray(k),
                            V_in=np.ascontiguousarray(v), tri=tri))
    return in_maps


def make_tri():
    tri = np.zeros((128, 4 * 512), dtype=np.float32)
    for u in range(4):
        p_idx = np.arange(128)[:, None]
        jj = np.arange(512)[None, :]
        tri[:, 512 * u:512 * (u + 1)] = (jj - 128 * u - p_idx >= 0).astype(np.float32)
    return tri.astype(bfdt)


def build_l0post():
    p = Prog("l0post")
    nc = p.nc
    i_x = nc.dram_tensor("x_T", [E, TPC], F32, kind="ExternalInput")
    i_xn = nc.dram_tensor("xn_bf", [E, TPC], BF16, kind="ExternalInput")
    i_ao = nc.dram_tensor("ao_bf", [E, TPC], BF16, kind="ExternalInput")
    i_wo = nc.dram_tensor("wout_bf", [E, E], BF16, kind="ExternalInput")
    i_g = nc.dram_tensor("g0_bf", [I0, E], BF16, kind="ExternalInput")
    i_u = nc.dram_tensor("u0_bf", [I0, E], BF16, kind="ExternalInput")
    i_dT = nc.dram_tensor("d0T_bf", [I0, E], BF16, kind="ExternalInput")
    o_x1 = nc.dram_tensor("x1_T", [E, TPC], F32, kind="ExternalOutput")

    NT, TT, EC, NI = 4, 512, 16, 43
    GS = 4
    NGRP = (NI + GS - 1) // GS
    grp_of = [(gi, list(range(GS * gi, min(GS * gi + GS, NI)))) for gi in range(NGRP)]
    gchunks = []
    tot = 0
    for tt in range(NT):
        for (gi, ics) in grp_of:
            tot += len(ics)
            gchunks.append(tot)
    WGA_PER = 32 * 16

    s_xn = p.sb("s_xn", [128, EC * TT], BF16)
    s_ao = p.sb("s_ao", [128, EC * TT], BF16)
    s_wgu = p.sb("s_wgu", [128, 2 * EC * 1024], BF16)
    s_wd = p.sb("s_wd", [128, 2 * NI * 128], BF16)
    s_sg = p.sb("s_sg", [128, 2 * TT], BF16)
    s_h = p.sb("s_h", [128, NI * TT], BF16)
    s_xs = p.sb("s_xs", [128, 2 * TT], F32)
    s_t1 = p.sb("s_t1", [128, TT], F32)
    s_x1 = p.sb("s_x1", [128, 2 * TT], F32)

    p_g = p.ps("p_g", [128, 1024], F32)
    p_u = p.ps("p_u", [128, 1024], F32)
    p_f = p.ps("p_f", [128, 1024], F32)
    p_a = p.ps("p_a", [128, 1024], F32)

    xns = [p.sem(f"xns{i}") for i in range(2)]
    aos = [p.sem(f"aos{i}") for i in range(2)]
    wos = [p.sem(f"wos{i}") for i in range(2)]
    wga = [p.sem(f"wga{i}") for i in range(2)]
    wds = [p.sem(f"wds{i}") for i in range(2)]
    xsl = [p.sem(f"xsl{i}") for i in range(2)]
    gum = p.sem("gum")
    sil = p.sem("sil")
    hev = p.sem("hev")
    fam = p.sem("fam")
    x1e = p.sem("x1e")
    x1s = [p.sem(f"x1s{i}") for i in range(2)]

    block = p.block()

    @block.sync
    def _(sy):
        for tt in range(NT):
            if tt >= 1:
                sy.wait_ge(gum.h, NI * tt)
                sy.wait_ge(fam.h, 16 * tt)
            for ec in range(EC):
                xns[tt % 2].inc(sy.dma_start(s_xn[:, TT * ec:TT * (ec + 1)],
                                             i_xn[128 * ec:128 * (ec + 1), TT * tt:TT * (tt + 1)]), 16)
                aos[tt % 2].inc(sy.dma_start(s_ao[:, TT * ec:TT * (ec + 1)],
                                             i_ao[128 * ec:128 * (ec + 1), TT * tt:TT * (tt + 1)]), 16)
            for (gi, ics) in grp_of:
                n = tt * NGRP + gi
                if gi == 0 and tt >= 1:
                    sy.wait_ge(fam.h, 16 * tt)
                if n >= 2:
                    sy.wait_ge(gum.h, gchunks[n - 2])
                b = (n % 2) * EC * 1024
                w_i = len(ics) * 128
                for ec in range(EC):
                    wga[n % 2].inc(sy.dma_start_transpose(
                        s_wgu[:, b + 1024 * ec:b + 1024 * ec + w_i],
                        i_g[512 * gi:512 * gi + w_i, 128 * ec:128 * (ec + 1)]), 16)
                    wga[n % 2].inc(sy.dma_start_transpose(
                        s_wgu[:, b + 1024 * ec + 512:b + 1024 * ec + 512 + w_i],
                        i_u[512 * gi:512 * gi + w_i, 128 * ec:128 * (ec + 1)]), 16)
            sy.wait_ge(gum.h, NI * (tt + 1))
            for ec in range(EC):
                for fg in range(4):
                    wos[tt % 2].inc(sy.dma_start_transpose(
                        s_wgu[:, 2048 * ec + 512 * fg:2048 * ec + 512 * (fg + 1)],
                        i_wo[512 * fg:512 * (fg + 1), 128 * ec:128 * (ec + 1)]), 16)

    @block.gpsimd
    def _(g):
        NQ = NT * 16
        for q in range(NQ + 2):
            if q < NQ:
                tt, fc = q // 16, q % 16
                if q >= 2:
                    g.wait_ge(fam.h, q - 1)
                wds[q % 2].inc(g.dma_start(
                    s_wd[:, (q % 2) * NI * 128:(q % 2) * NI * 128 + NI * 128].rearrange(
                        "p (ic f) -> p ic f", f=128),
                    i_dT[:, 128 * fc:128 * (fc + 1)].rearrange("(ic p) f -> p ic f", p=128)), 16)
                if q >= 2:
                    g.wait_ge(x1e.h, q - 1)
                xsl[q % 2].inc(g.dma_start(s_xs[:, (q % 2) * TT:(q % 2 + 1) * TT],
                                           i_x[128 * fc:128 * (fc + 1), TT * tt:TT * (tt + 1)]), 16)
            if q >= 2:
                qs = q - 2
                ts2, fs2 = qs // 16, qs % 16
                g.wait_ge(x1e.h, qs + 1)
                x1s[qs % 2].inc(g.dma_start(o_x1[128 * fs2:128 * (fs2 + 1), TT * ts2:TT * (ts2 + 1)],
                                            s_x1[:, (qs % 2) * TT:(qs % 2 + 1) * TT]), 16)
        g.wait_ge(x1s[0].h, x1s[0].n)
        g.wait_ge(x1s[1].h, x1s[1].n)

    @block.tensor
    def _(t):
        for tt in range(NT):
            t.wait_ge(xns[tt % 2].h, 16 * EC * (tt // 2 + 1))
            if tt >= 1:
                t.wait_ge(wos[(tt - 1) % 2].h, 16 * 64 * ((tt - 1) // 2 + 1))
            for (gi, ics) in grp_of:
                n = tt * NGRP + gi
                t.wait_ge(wga[n % 2].h, WGA_PER * (n // 2 + 1))
                b = (n % 2) * EC * 1024
                for li, ic in enumerate(ics):
                    icg = tt * NI + ic
                    if icg >= 2:
                        t.wait_ge(sil.h, icg - 1)
                        t.wait_ge(hev.h, icg - 1)
                    for ec in range(EC):
                        t.matmul(p_g[:, (ic % 2) * 512:(ic % 2) * 512 + 512],
                                 s_wgu[:, b + 1024 * ec + 128 * li:b + 1024 * ec + 128 * (li + 1)],
                                 s_xn[:, TT * ec:TT * (ec + 1)],
                                 start=(ec == 0), stop=(ec == EC - 1))
                    for ec in range(EC):
                        mm = t.matmul(p_u[:, (ic % 2) * 512:(ic % 2) * 512 + 512],
                                      s_wgu[:, b + 1024 * ec + 512 + 128 * li:b + 1024 * ec + 512 + 128 * (li + 1)],
                                      s_xn[:, TT * ec:TT * (ec + 1)],
                                      start=(ec == 0), stop=(ec == EC - 1))
                    gum.inc(mm, 1)
            t.wait_ge(hev.h, NI * (tt + 1))
            t.wait_ge(wos[tt % 2].h, 16 * 64 * (tt // 2 + 1))
            t.wait_ge(aos[tt % 2].h, 16 * EC * (tt // 2 + 1))
            for fc in range(16):
                q = tt * 16 + fc
                t.wait_ge(wds[q % 2].h, 16 * (q // 2 + 1))
                if q >= 2:
                    t.wait_ge(x1e.h, q - 1)
                for ic in range(NI):
                    t.matmul(p_f[:, (q % 2) * 512:(q % 2) * 512 + 512],
                             s_wd[:, (q % 2) * NI * 128 + 128 * ic:(q % 2) * NI * 128 + 128 * (ic + 1)],
                             s_h[:, TT * ic:TT * (ic + 1)],
                             start=(ic == 0), stop=(ic == NI - 1))
                for ec in range(EC):
                    mm = t.matmul(p_a[:, (q % 2) * 512:(q % 2) * 512 + 512],
                                  s_wgu[:, 2048 * ec + 128 * fc:2048 * ec + 128 * (fc + 1)],
                                  s_ao[:, TT * ec:TT * (ec + 1)],
                                  start=(ec == 0), stop=(ec == EC - 1))
                fam.inc(mm, 1)

    @block.scalar
    def _(sc):
        for tt in range(NT):
            for ic in range(NI):
                icg = tt * NI + ic
                sc.wait_ge(gum.h, icg + 1)
                if icg >= 2:
                    sc.wait_ge(hev.h, icg - 1)
                sil.inc(sc.activation(s_sg[:, (ic % 2) * TT:(ic % 2) * TT + TT],
                                      p_g[:, (ic % 2) * 512:(ic % 2) * 512 + 512],
                                      AF.Silu), 1)

    @block.vector
    def _(v):
        for tt in range(NT):
            for ic in range(NI):
                icg = tt * NI + ic
                v.wait_ge(sil.h, icg + 1)
                if tt >= 1 and ic == 0:
                    v.wait_ge(fam.h, 16 * tt)
                hev.inc(v.tensor_mul(s_h[:, TT * ic:TT * (ic + 1)],
                                     s_sg[:, (ic % 2) * TT:(ic % 2) * TT + TT],
                                     p_u[:, (ic % 2) * 512:(ic % 2) * 512 + 512]), 1)
            for fc in range(16):
                q = tt * 16 + fc
                v.wait_ge(fam.h, q + 1)
                v.wait_ge(xsl[q % 2].h, 16 * (q // 2 + 1))
                if q >= 2:
                    v.wait_ge(x1s[q % 2].h, 16 * (q // 2))
                v.tensor_add(s_t1[:, :], s_xs[:, (q % 2) * TT:(q % 2 + 1) * TT],
                             p_a[:, (q % 2) * 512:(q % 2) * 512 + 512])
                x1e.inc(v.tensor_add(s_x1[:, (q % 2) * TT:(q % 2 + 1) * TT],
                                     s_t1[:, :],
                                     p_f[:, (q % 2) * 512:(q % 2) * 512 + 512]), 1)

    p.close()
    return p.nc


def host_l0post(inputs, w, l2, l3):
    x = np.asarray(inputs["L0_attn_hidden_states"]).reshape(B * S, E)
    in_maps = []
    for c in range(NCORE):
        b = c // 2
        ao = np.concatenate([l3[2 * b + hh // 8]["AO"][hh % 8] for hh in range(16)], axis=0)
        ao = ao[:, (c % 2) * TPC:(c % 2 + 1) * TPC]
        in_maps.append(dict(
            x_T=np.ascontiguousarray(x[c * TPC:(c + 1) * TPC].T),
            xn_bf=l2[c]["o_xn"],
            ao_bf=np.ascontiguousarray(ao),
            wout_bf=w["wout"], g0_bf=w["g0"], u0_bf=w["u0"], d0T_bf=w["d0T"],
        ))
    return in_maps


def build_l1a():
    p = Prog("l1a")
    nc = p.nc
    i_x = nc.dram_tensor("x1_T", [E, TPC], F32, kind="ExternalInput")
    i_wq = nc.dram_tensor("l1q_bf", [E, E], BF16, kind="ExternalInput")
    i_wk = nc.dram_tensor("l1k_bf", [E, E], BF16, kind="ExternalInput")
    i_wv = nc.dram_tensor("l1v_bf", [E, E], BF16, kind="ExternalInput")
    i_qb = nc.dram_tensor("qb_row", [1, E], F32, kind="ExternalInput")
    i_kb = nc.dram_tensor("kb_row", [1, E], F32, kind="ExternalInput")
    i_vb = nc.dram_tensor("vb_row", [1, E], F32, kind="ExternalInput")
    o_xn1 = nc.dram_tensor("o_xn1", [E, TPC], BF16, kind="ExternalOutput")
    o_qp = nc.dram_tensor("o_qp", [E, TPC], BF16, kind="ExternalOutput")
    o_kvp = nc.dram_tensor("o_kvp", [128, 2048], F32, kind="ExternalOutput")
    o_ksp = nc.dram_tensor("o_ksp", [1, 2048], F32, kind="ExternalOutput")
    n_kp = nc.dram_tensor("n_kp", [TPC, E], BF16, kind="Internal")
    n_v = nc.dram_tensor("n_v", [TPC, E], BF16, kind="Internal")

    NT, TT, EC = 4, 512, 16

    s_xc = p.sb("s_xc", [128, 4 * TT], F32)
    s_xn = p.sb("s_xn", [128, EC * TPC], BF16)
    s_sq = p.sb("s_sq", [128, 2 * TT], BF16)
    s_rows = p.sb("s_rows", [1, 5 * TT], F32)
    s_rbf = p.sb("s_rbf", [1, 2 * TT], BF16)
    s_cst = p.sb("s_cst", [1, 2], F32)
    s_ones = p.sb("s_ones", [128, 1], BF16)
    s_onr = p.sb("s_onr", [1, 128], BF16)
    s_onr5 = p.sb("s_onr5", [1, 512], BF16)
    s_w = p.sb("s_w", [128, 2 * EC * 512], BF16)
    s_biasf = p.sb("s_biasf", [1, E], F32)
    s_bias = p.sb("s_bias", [1, 3 * E], BF16)
    s_tmin = p.sb("s_tmin", [128, 2 * TT], F32)
    s_texp = p.sb("s_texp", [128, 2 * TT], BF16)
    s_qp = p.sb("s_qp", [128, 2 * TT], BF16)
    s_kp = p.sb("s_kp", [128, 2 * TT], BF16)
    s_vv = p.sb("s_vv", [128, 2 * TT], BF16)
    s_kt = p.sb("s_kt", [128, 2 * E], BF16)
    s_vt = p.sb("s_vt", [128, 2 * E], BF16)
    s_kv = p.sb("s_kv", [128, E], F32)
    s_ks = p.sb("s_ks", [1, E], F32)
    s_tmp = p.sb("s_tmp", [128, 2 * TT], F32)

    p_mm = p.ps("p_mm", [128, 2048], F32)
    p_bc = p.ps("p_bc", [128, 1024], F32)
    p_kx = p.ps("p_kx", [128, 1024], F32)

    xld = [p.sem(f"xld{i}") for i in range(4)]
    misc = p.sem("misc")
    bcv = p.sem("bcv")
    onesd = p.sem("onesd")
    vxb = p.sem("vxb")
    sqs = p.sem("sqs")
    pesq = p.sem("pesq")
    rowv = p.sem("rowv")
    rowr = p.sem("rowr")
    rowb = p.sem("rowb")
    bcm = p.sem("bcm")
    xnst = p.sem("xnst")
    wqs = [p.sem(f"wqs{i}") for i in range(2)]
    qmm = p.sem("qmm")
    qe1 = p.sem("qe1")
    qe2 = p.sem("qe2")
    qe3 = p.sem("qe3")
    qst = [p.sem(f"qst{i}") for i in range(2)]
    wkv = p.sem("wkv")
    kvm = p.sem("kvm")
    ke1 = p.sem("ke1")
    ke2 = p.sem("ke2")
    ke3 = p.sem("ke3")
    vcp = p.sem("vcp")
    kst = [p.sem(f"kst{i}") for i in range(2)]
    ktl = [p.sem(f"ktl{i}") for i in range(2)]
    c2m = p.sem("c2m")
    c2v = p.sem("c2v")
    fst = p.sem("fst")

    block = p.block()

    @block.sync
    def _(sy):
        for bi, ib in enumerate((i_qb, i_kb, i_vb)):
            if bi >= 1:
                sy.wait_ge(bcv.h, bi)
            misc.inc(sy.dma_start(s_biasf[:, :], ib[:, :]), 16)
        for tt in range(NT):
            for ps in range(2):
                for ec in range(EC):
                    k = tt * 32 + ps * 16 + ec
                    if k >= 4:
                        sy.wait_ge(vxb.h, k - 3)
                    xld[k % 4].inc(sy.dma_start(s_xc[:, TT * (k % 4):TT * (k % 4 + 1)],
                                                i_x[128 * ec:128 * (ec + 1), TT * tt:TT * (tt + 1)]), 16)
        for fg in range(4):
            if fg >= 2:
                sy.wait_ge(qmm.h, 16 * (fg - 1))
            b = (fg % 2) * EC * 512
            for ec in range(EC):
                wqs[fg % 2].inc(sy.dma_start_transpose(
                    s_w[:, b + 512 * ec:b + 512 * (ec + 1)],
                    i_wq[512 * fg:512 * (fg + 1), 128 * ec:128 * (ec + 1)]), 16)
        for fg in range(4):
            if fg == 0:
                sy.wait_ge(qmm.h, 64)
            else:
                sy.wait_ge(kvm.h, 16 * fg)
            for ec in range(EC):
                wkv.inc(sy.dma_start_transpose(
                    s_w[:, 512 * ec:512 * (ec + 1)],
                    i_wk[512 * fg:512 * (fg + 1), 128 * ec:128 * (ec + 1)]), 16)
                wkv.inc(sy.dma_start_transpose(
                    s_w[:, EC * 512 + 512 * ec:EC * 512 + 512 * (ec + 1)],
                    i_wv[512 * fg:512 * (fg + 1), 128 * ec:128 * (ec + 1)]), 16)

    @block.vector
    def _(v):
        v.memset(s_ones[:, :], 1.0)
        v.memset(s_cst[:, 0:1], 1e-5)
        v.memset(s_cst[:, 1:2], -0.5)
        v.memset(s_onr5[:, :], 1.0)
        v.memset(s_kv[:, :], 0.0)
        v.memset(s_ks[:, :], 0.0)
        for bi in range(3):
            v.wait_ge(misc.h, 16 * (bi + 1))
            bcv.inc(v.tensor_copy(s_bias[:, bi * E:(bi + 1) * E], s_biasf[:, :]), 1)
        onesd.inc(v.memset(s_onr[:, :], 1.0), 1)
        for tt in range(NT):
            for ec in range(EC):
                k = tt * 32 + ec
                v.wait_ge(xld[k % 4].h, 16 * (k // 4 + 1))
                vxb.inc(v.tensor_copy(s_xn[:, TPC * ec + TT * tt: TPC * ec + TT * (tt + 1)],
                                      s_xc[:, TT * (k % 4):TT * (k % 4 + 1)]), 1)
            v.wait_ge(pesq.h, 16 * (tt + 1))
            v.tensor_scalar(s_rows[:, 0:TT], p_bc[0:1, 0:TT], 1.0 / E, None, ALU.mult)
            v.tensor_scalar(s_rows[:, TT:2 * TT], p_bc[0:1, 512:512 + TT], 1.0 / E, None, ALU.mult)
            v.tensor_mul(s_rows[:, 3 * TT:4 * TT], s_rows[:, 0:TT], s_rows[:, 0:TT])
            rowv.inc(v.tensor_sub(s_rows[:, 2 * TT:3 * TT], s_rows[:, TT:2 * TT],
                                  s_rows[:, 3 * TT:4 * TT]), 1)
            v.wait_ge(rowr.h, tt + 1)
            v.tensor_copy(s_rbf[:, 0:TT], s_rows[:, 0:TT])
            rowb.inc(v.tensor_copy(s_rbf[:, TT:2 * TT], s_rows[:, 4 * TT:5 * TT]), 1)
            v.wait_ge(bcm.h, 2 * (tt + 1))
            for ec in range(EC):
                k = tt * 32 + 16 + ec
                v.wait_ge(xld[k % 4].h, 16 * (k // 4 + 1))
                bsel = ec % 2
                v.tensor_sub(s_tmp[:, bsel * TT:(bsel + 1) * TT],
                             s_xc[:, TT * (k % 4):TT * (k % 4 + 1)], p_bc[:, 0:TT])
                last = v.tensor_mul(s_xn[:, TPC * ec + TT * tt: TPC * ec + TT * (tt + 1)],
                                    s_tmp[:, bsel * TT:(bsel + 1) * TT], p_bc[:, 512:512 + TT])
                vxb.inc(last, 1)
        for u in range(64):
            v.wait_ge(qmm.h, u + 1)
            qe1.inc(v.tensor_scalar_min(s_tmin[:, (u % 2) * TT:(u % 2 + 1) * TT],
                                        p_mm[:, (u % 2) * 512:(u % 2) * 512 + 512], 0.0), 1)
            v.wait_ge(qe2.h, u + 1)
            if u >= 2:
                v.wait_ge(qst[u % 2].h, 16 * (u // 2))
            qe3.inc(v.scalar_tensor_tensor(s_qp[:, (u % 2) * TT:(u % 2 + 1) * TT],
                                           p_mm[:, (u % 2) * 512:(u % 2) * 512 + 512],
                                           0.0, s_texp[:, (u % 2) * TT:(u % 2 + 1) * TT],
                                           ALU.max, ALU.add), 1)
        for u in range(64):
            v.wait_ge(kvm.h, u + 1)
            ke1.inc(v.tensor_scalar_min(s_tmin[:, (u % 2) * TT:(u % 2 + 1) * TT],
                                        p_mm[:, (u % 2) * 512:(u % 2) * 512 + 512], 0.0), 1)
            v.wait_ge(ke2.h, u + 1)
            if u >= 2:
                v.wait_ge(kst[u % 2].h, 32 * (u // 2))
            ke3.inc(v.scalar_tensor_tensor(s_kp[:, (u % 2) * TT:(u % 2 + 1) * TT],
                                           p_mm[:, (u % 2) * 512:(u % 2) * 512 + 512],
                                           0.0, s_texp[:, (u % 2) * TT:(u % 2 + 1) * TT],
                                           ALU.max, ALU.add), 1)
            vcp.inc(v.tensor_copy(s_vv[:, (u % 2) * TT:(u % 2 + 1) * TT],
                                  p_mm[:, 1024 + (u % 2) * 512:1024 + (u % 2) * 512 + 512]), 1)
        for tc in range(16):
            for h in range(H):
                cnt = tc * 20 + h
                v.wait_ge(c2m.h, cnt + 1)
                c2v.inc(v.tensor_add(s_kv[:, 128 * h:128 * (h + 1)],
                                     s_kv[:, 128 * h:128 * (h + 1)],
                                     p_kx[:, (cnt % 2) * 512:(cnt % 2) * 512 + 128]), 1)
            for fg in range(4):
                cnt = tc * 20 + 16 + fg
                v.wait_ge(c2m.h, cnt + 1)
                c2v.inc(v.tensor_add(s_ks[:, 512 * fg:512 * (fg + 1)],
                                     s_ks[:, 512 * fg:512 * (fg + 1)],
                                     p_bc[0:1, (cnt % 2) * 512:(cnt % 2) * 512 + 512]), 1)

    @block.scalar
    def _(sc):
        for tt in range(NT):
            for ec in range(EC):
                sc.wait_ge(vxb.h, 32 * tt + ec + 1)
                if 16 * tt + ec >= 2:
                    sc.wait_ge(pesq.h, 16 * tt + ec - 1)
                sqs.inc(sc.activation(s_sq[:, (ec % 2) * TT:(ec % 2 + 1) * TT],
                                      s_xn[:, TPC * ec + TT * tt: TPC * ec + TT * (tt + 1)],
                                      AF.Square), 1)
            sc.wait_ge(rowv.h, tt + 1)
            sc.wait_ge(onesd.h, 1)
            sc.activation(s_rows[:, 3 * TT:4 * TT], s_rows[:, 2 * TT:3 * TT], AF.Ln,
                          bias=s_cst[0:1, 0:1])
            rowr.inc(sc.activation(s_rows[:, 4 * TT:5 * TT], s_rows[:, 3 * TT:4 * TT],
                                   AF.Exp, scale=s_cst[0:1, 1:2]), 1)
        for u in range(64):
            sc.wait_ge(qe1.h, u + 1)
            qe2.inc(sc.activation(s_texp[:, (u % 2) * TT:(u % 2 + 1) * TT],
                                  s_tmin[:, (u % 2) * TT:(u % 2 + 1) * TT], AF.Exp), 1)
        for u in range(64):
            sc.wait_ge(ke1.h, u + 1)
            ke2.inc(sc.activation(s_texp[:, (u % 2) * TT:(u % 2 + 1) * TT],
                                  s_tmin[:, (u % 2) * TT:(u % 2 + 1) * TT], AF.Exp), 1)

    @block.tensor
    def _(t):
        t.wait_ge(bcv.h, 3)
        t.wait_ge(onesd.h, 1)
        for tt in range(NT):
            if tt >= 1:
                t.wait_ge(vxb.h, 32 * tt)
            for ec in range(EC):
                t.wait_ge(vxb.h, 32 * tt + ec + 1)
                t.matmul(p_bc[0:1, 0:512], s_ones[:, :],
                         s_xn[:, TPC * ec + TT * tt: TPC * ec + TT * (tt + 1)],
                         start=(ec == 0), stop=(ec == EC - 1))
                t.wait_ge(sqs.h, 16 * tt + ec + 1)
                mm2 = t.matmul(p_bc[0:1, 512:1024], s_ones[:, :],
                               s_sq[:, (ec % 2) * TT:(ec % 2 + 1) * TT],
                               start=(ec == 0), stop=(ec == EC - 1))
                pesq.inc(mm2, 1)
            t.wait_ge(rowb.h, tt + 1)
            bcm.inc(t.matmul(p_bc[:, 0:512], s_onr[:, :], s_rbf[:, 0:TT],
                             start=True, stop=True), 1)
            bcm.inc(t.matmul(p_bc[:, 512:1024], s_onr[:, :], s_rbf[:, TT:2 * TT],
                             start=True, stop=True), 1)
        t.wait_ge(vxb.h, 32 * NT)
        for fg in range(4):
            t.wait_ge(wqs[fg % 2].h, 16 * EC * (fg // 2 + 1))
            for fc in range(4):
                for tt in range(NT):
                    u = fg * 16 + fc * 4 + tt
                    if u >= 2:
                        t.wait_ge(qe3.h, u - 1)
                    b = (fg % 2) * EC * 512
                    fglob = 512 * fg + 128 * fc
                    t.matmul(p_mm[:, (u % 2) * 512:(u % 2) * 512 + 512],
                             s_bias[0:1, fglob:fglob + 128], s_onr5[:, :],
                             start=True, stop=False)
                    for ec in range(EC):
                        mm = t.matmul(p_mm[:, (u % 2) * 512:(u % 2) * 512 + 512],
                                      s_w[:, b + 512 * ec + 128 * fc:b + 512 * ec + 128 * (fc + 1)],
                                      s_xn[:, TPC * ec + TT * tt: TPC * ec + TT * (tt + 1)],
                                      start=False, stop=(ec == EC - 1))
                    qmm.inc(mm, 1)
        for fg in range(4):
            t.wait_ge(wkv.h, 16 * 2 * EC * (fg + 1))
            for tc in range(16):
                u = fg * 16 + tc
                if u >= 2:
                    t.wait_ge(ke3.h, u - 1)
                    t.wait_ge(vcp.h, u - 1)
                fglob = 512 * fg
                t.matmul(p_mm[:, (u % 2) * 512:(u % 2) * 512 + 512],
                         s_onr[0:1, 0:128], s_bias[0:1, E + fglob:E + fglob + 512],
                         start=True, stop=False)
                t.matmul(p_mm[:, 1024 + (u % 2) * 512:1024 + (u % 2) * 512 + 512],
                         s_onr[0:1, 0:128], s_bias[0:1, 2 * E + fglob:2 * E + fglob + 512],
                         start=True, stop=False)
                for ec in range(EC):
                    t.matmul(p_mm[:, (u % 2) * 512:(u % 2) * 512 + 512],
                             s_xn[:, TPC * ec + 128 * tc:TPC * ec + 128 * (tc + 1)],
                             s_w[:, 512 * ec:512 * (ec + 1)],
                             start=False, stop=(ec == EC - 1))
                for ec in range(EC):
                    mm = t.matmul(p_mm[:, 1024 + (u % 2) * 512:1024 + (u % 2) * 512 + 512],
                                  s_xn[:, TPC * ec + 128 * tc:TPC * ec + 128 * (tc + 1)],
                                  s_w[:, EC * 512 + 512 * ec:EC * 512 + 512 * (ec + 1)],
                                  start=False, stop=(ec == EC - 1))
                kvm.inc(mm, 1)
        for tc in range(16):
            t.wait_ge(ktl[tc % 2].h, 32 * (tc // 2 + 1))
            for h in range(H):
                cnt = tc * 20 + h
                if cnt >= 2:
                    t.wait_ge(c2v.h, cnt - 1)
                c2m.inc(t.matmul(p_kx[:, (cnt % 2) * 512:(cnt % 2) * 512 + 128],
                                 s_kt[:, (tc % 2) * E + 128 * h:(tc % 2) * E + 128 * (h + 1)],
                                 s_vt[:, (tc % 2) * E + 128 * h:(tc % 2) * E + 128 * (h + 1)],
                                 start=True, stop=True), 1)
            for fg in range(4):
                cnt = tc * 20 + 16 + fg
                if cnt >= 2:
                    t.wait_ge(c2v.h, cnt - 1)
                c2m.inc(t.matmul(p_bc[0:1, (cnt % 2) * 512:(cnt % 2) * 512 + 512],
                                 s_ones[:, :],
                                 s_kt[:, (tc % 2) * E + 512 * fg:(tc % 2) * E + 512 * (fg + 1)],
                                 start=True, stop=True), 1)

    @block.gpsimd
    def _(g):
        g.wait_ge(vxb.h, 32 * NT)
        for ec in range(EC):
            xnst.inc(g.dma_start(o_xn1[128 * ec:128 * (ec + 1), :],
                                 s_xn[:, TPC * ec:TPC * (ec + 1)]), 16)
        for u in range(64):
            fg, fc, tt = u // 16, (u % 16) // 4, u % 4
            g.wait_ge(qe3.h, u + 1)
            qst[u % 2].inc(g.dma_start(o_qp[512 * fg + 128 * fc:512 * fg + 128 * (fc + 1),
                                            TT * tt:TT * (tt + 1)],
                                       s_qp[:, (u % 2) * TT:(u % 2 + 1) * TT]), 16)
        for u in range(64):
            fg, tc = u // 16, u % 16
            g.wait_ge(ke3.h, u + 1)
            kst[u % 2].inc(g.dma_start(n_kp[128 * tc:128 * (tc + 1), 512 * fg:512 * (fg + 1)],
                                       s_kp[:, (u % 2) * TT:(u % 2 + 1) * TT]), 16)
            g.wait_ge(vcp.h, u + 1)
            kst[u % 2].inc(g.dma_start(n_v[128 * tc:128 * (tc + 1), 512 * fg:512 * (fg + 1)],
                                       s_vv[:, (u % 2) * TT:(u % 2 + 1) * TT]), 16)
        for tc in range(16):
            if tc >= 2:
                g.wait_ge(c2v.h, 20 * (tc - 1))
            ktl[tc % 2].inc(g.dma_start(s_kt[:, (tc % 2) * E:(tc % 2 + 1) * E],
                                        n_kp[128 * tc:128 * (tc + 1), :]), 16)
            ktl[tc % 2].inc(g.dma_start(s_vt[:, (tc % 2) * E:(tc % 2 + 1) * E],
                                        n_v[128 * tc:128 * (tc + 1), :]), 16)
        g.wait_ge(c2v.h, 20 * 16)
        fst.inc(g.dma_start(o_kvp[:, :], s_kv[:, :]), 16)
        fst.inc(g.dma_start(o_ksp[:, :], s_ks[:, :]), 16)
        g.wait_ge(xnst.h, 16 * EC)
        g.wait_ge(qst[0].h, qst[0].n)
        g.wait_ge(qst[1].h, qst[1].n)
        g.wait_ge(fst.h, 32)

    p.close()
    return p.nc


def host_l1a(inputs, l4, w):
    in_maps = []
    for c in range(NCORE):
        in_maps.append(dict(
            x1_T=l4[c]["x1_T"],
            l1q_bf=w["l1q"], l1k_bf=w["l1k"], l1v_bf=w["l1v"],
            qb_row=np.asarray(inputs["L1_attn_q_b"]).reshape(1, E),
            kb_row=np.asarray(inputs["L1_attn_k_b"]).reshape(1, E),
            vb_row=np.asarray(inputs["L1_attn_v_b"]).reshape(1, E),
        ))
    return in_maps


def build_l1post():
    p = Prog("l1post")
    nc = p.nc
    i_x = nc.dram_tensor("x1_T", [E, TPC], F32, kind="ExternalInput")
    i_xn = nc.dram_tensor("xn1_bf", [E, TPC], BF16, kind="ExternalInput")
    i_qp = nc.dram_tensor("qp_bf", [E, TPC], BF16, kind="ExternalInput")
    i_kv0 = nc.dram_tensor("kvp0", [128, E], F32, kind="ExternalInput")
    i_kv1 = nc.dram_tensor("kvp1", [128, E], F32, kind="ExternalInput")
    i_ks0 = nc.dram_tensor("ksp0", [128, 16], F32, kind="ExternalInput")
    i_ks1 = nc.dram_tensor("ksp1", [128, 16], F32, kind="ExternalInput")
    i_wo = nc.dram_tensor("l1o_bf", [E, E], BF16, kind="ExternalInput")
    i_ob = nc.dram_tensor("ob_row", [1, E], F32, kind="ExternalInput")
    i_g = nc.dram_tensor("g1_bf", [I1, E], BF16, kind="ExternalInput")
    i_u = nc.dram_tensor("u1_bf", [I1, E], BF16, kind="ExternalInput")
    i_dT = nc.dram_tensor("d1T_bf", [I1, E], BF16, kind="ExternalInput")
    n_kv = nc.dram_tensor("n_kv", [128, E], F32, kind="Internal")
    i_gs = nc.dram_tensor("gs_pc", [128, 64], F32, kind="ExternalInput")
    i_us = nc.dram_tensor("us_pc", [128, 64], F32, kind="ExternalInput")
    i_ds = nc.dram_tensor("ds_pc", [128, 16], F32, kind="ExternalInput")
    o_xo = nc.dram_tensor("xo_T", [E, TPC], F32, kind="ExternalOutput")

    NT, TT, EC, NI = 4, 512, 16, 64
    GS = 4
    NGRP = NI // GS
    WGA_PER = 32 * 16

    s_xn = p.sb("s_xn", [128, EC * TT], BF16)
    s_attn = p.sb("s_attn", [128, EC * TT], BF16)
    s_wgu = p.sb("s_wgu", [128, 2 * EC * 1024], BF16)
    s_wd = p.sb("s_wd", [128, NI * 128], BF16)
    s_h = p.sb("s_h", [128, NI * TT], BF16)
    s_sg = p.sb("s_sg", [128, 2 * TT], BF16)
    s_qp = p.sb("s_qp", [128, 2 * TT], BF16)
    s_kvt = p.sb("s_kvt", [128, E], BF16)
    s_ksf = p.sb("s_ksf", [128, 16], F32)
    s_ksb = p.sb("s_ksb", [128, 16], BF16)
    s_obb = p.sb("s_obb", [1, E], BF16)
    s_gs = p.sb("s_gs", [128, 64], F32)
    s_us = p.sb("s_us", [128, 64], F32)
    s_ds = p.sb("s_ds", [128, 16], F32)
    s_zr = p.sb("s_zr", [1, 512], F32)
    s_zb = p.sb("s_zb", [1, 512], BF16)
    s_zbc = p.sb("s_zbc", [128, 2 * 512], F32)
    s_xs = p.sb("s_xs", [128, 2 * TT], F32)
    s_t1 = p.sb("s_t1", [128, TT], F32)
    s_x2 = p.sb("s_x2", [128, 2 * TT], F32)
    s_onr = p.sb("s_onr", [1, 128], BF16)
    s_onr5 = p.sb("s_onr5", [1, 512], BF16)

    PA = p.ps("PA", [128, 2048], F32)
    PB = p.ps("PB", [128, 1024], F32)
    PC = p.ps("PC", [1, 1024], F32)

    misc = p.sem("misc")
    kva = p.sem("kva")
    kvr = p.sem("kvr")
    onesd = p.sem("onesd")
    xns = [p.sem(f"xns{i}") for i in range(2)]
    qpl = [p.sem(f"qpl{i}") for i in range(2)]
    wga = [p.sem(f"wga{i}") for i in range(2)]
    wos = [p.sem(f"wos{i}") for i in range(2)]
    wdl = p.sem("wdl")
    xsl = [p.sem(f"xsl{i}") for i in range(2)]
    amm = p.sem("amm")
    zrd = p.sem("zrd")
    zbcm = p.sem("zbcm")
    zcp = p.sem("zcp")
    aev = p.sem("aev")
    gum = p.sem("gum")
    sil = p.sem("sil")
    hev = p.sem("hev")
    fam = p.sem("fam")
    x2e = p.sem("x2e")
    x2s = [p.sem(f"x2s{i}") for i in range(2)]

    block = p.block()

    @block.sync
    def _(sy):
        misc.inc(sy.dma_start(s_zbc[0:1, 0:1024], i_ob[:, 0:1024]), 16)
        sy.wait_ge(kvr.h, 1)
        misc.inc(sy.dma_start(s_zbc[0:1, 0:1024], i_ob[:, 1024:2048]), 16)
        misc.inc(sy.dma_start(n_kv[:, :], i_kv0[:, :]), 16)
        misc.inc(sy.dma_start(s_ksf[:, :], i_ks0[:, :]), 16)
        misc.inc(sy.dma_start(s_gs[:, :], i_gs[:, :]), 16)
        misc.inc(sy.dma_start(s_us[:, :], i_us[:, :]), 16)
        misc.inc(sy.dma_start(s_ds[:, :], i_ds[:, :]), 16)
        for tt in range(NT):
            for h in range(H):
                ha = tt * H + h
                if ha >= 2:
                    sy.wait_ge(amm.h, ha - 1)
                qpl[ha % 2].inc(sy.dma_start(s_qp[:, (ha % 2) * TT:(ha % 2 + 1) * TT],
                                             i_qp[128 * h:128 * (h + 1), TT * tt:TT * (tt + 1)]), 16)
            if tt >= 1:
                sy.wait_ge(gum.h, NI * tt)
            for ec in range(EC):
                xns[tt % 2].inc(sy.dma_start(s_xn[:, TT * ec:TT * (ec + 1)],
                                             i_xn[128 * ec:128 * (ec + 1), TT * tt:TT * (tt + 1)]), 16)
            if tt == 0:
                sy.wait_ge(kva.h, 80)
            for gi in range(NGRP):
                n = tt * NGRP + gi
                if gi == 0 and tt >= 1:
                    sy.wait_ge(fam.h, 16 * tt)
                if n >= 2:
                    done = ((n - 2) // NGRP) * NI + min(GS * ((n - 2) % NGRP + 1), NI)
                    sy.wait_ge(gum.h, done)
                b = (n % 2) * EC * 1024
                for ec in range(EC):
                    wga[n % 2].inc(sy.dma_start_transpose(
                        s_wgu[:, b + 1024 * ec:b + 1024 * ec + 512],
                        i_g[512 * gi:512 * (gi + 1), 128 * ec:128 * (ec + 1)]), 16)
                    wga[n % 2].inc(sy.dma_start_transpose(
                        s_wgu[:, b + 1024 * ec + 512:b + 1024 * ec + 1024],
                        i_u[512 * gi:512 * (gi + 1), 128 * ec:128 * (ec + 1)]), 16)
            sy.wait_ge(gum.h, NI * (tt + 1))
            for ec in range(EC):
                for fg in range(4):
                    wos[tt % 2].inc(sy.dma_start_transpose(
                        s_wgu[:, 2048 * ec + 512 * fg:2048 * ec + 512 * (fg + 1)],
                        i_wo[512 * fg:512 * (fg + 1), 128 * ec:128 * (ec + 1)]), 16)

    @block.gpsimd
    def _(g):
        g.wait_ge(misc.h, 16 * 7)
        kva.inc(g.dma_start(n_kv[:, 0:1024], i_kv1[:, 0:1024], accum_op=ALU.add), 16)
        kva.inc(g.dma_start(n_kv[:, 1024:2048], i_kv1[:, 1024:2048], accum_op=ALU.add), 16)
        kva.inc(g.dma_start(s_ksf[:, :], i_ks1[:, :], accum_op=ALU.add), 16)
        g.wait_ge(kva.h, 48)
        kva.inc(g.dma_start(s_zbc[:, :], n_kv[:, 0:1024]), 16)
        g.wait_ge(kvr.h, 3)
        kva.inc(g.dma_start(s_zbc[:, :], n_kv[:, 1024:2048]), 16)
        NQ = NT * 16
        for q in range(NQ + 2):
            if q < NQ:
                tt, fc = q // 16, q % 16
                g.wait_ge(fam.h, q)
                wdl.inc(g.dma_start(
                    s_wd[:, :].rearrange("p (ic f) -> p ic f", f=128),
                    i_dT[:, 128 * fc:128 * (fc + 1)].rearrange("(ic p) f -> p ic f", p=128)), 16)
                if q >= 2:
                    g.wait_ge(x2e.h, q - 1)
                xsl[q % 2].inc(g.dma_start(s_xs[:, (q % 2) * TT:(q % 2 + 1) * TT],
                                           i_x[128 * fc:128 * (fc + 1), TT * tt:TT * (tt + 1)]), 16)
            if q >= 2:
                qs = q - 2
                ts2, fs2 = qs // 16, qs % 16
                g.wait_ge(x2e.h, qs + 1)
                x2s[qs % 2].inc(g.dma_start(o_xo[128 * fs2:128 * (fs2 + 1), TT * ts2:TT * (ts2 + 1)],
                                            s_x2[:, (qs % 2) * TT:(qs % 2 + 1) * TT]), 16)
        g.wait_ge(x2s[0].h, x2s[0].n)
        g.wait_ge(x2s[1].h, x2s[1].n)

    @block.tensor
    def _(t):
        t.wait_ge(kvr.h, 4)
        t.wait_ge(onesd.h, 1)
        for tt in range(NT):
            for h in range(H):
                ha = tt * H + h
                t.wait_ge(qpl[ha % 2].h, 16 * (ha // 2 + 1))
                if ha >= 2:
                    t.wait_ge(aev.h, ha - 1)
                    t.wait_ge(zrd.h, ha - 1)
                    t.wait_ge(zcp.h, ha - 1)
                if tt >= 1 and h == 0:
                    t.wait_ge(x2e.h, 16 * tt)
                t.matmul(PA[:, (ha % 2) * 512:(ha % 2) * 512 + 512],
                         s_kvt[:, 128 * h:128 * (h + 1)],
                         s_qp[:, (ha % 2) * TT:(ha % 2 + 1) * TT],
                         start=True, stop=True)
                amm.inc(t.matmul(PC[0:1, (ha % 2) * 512:(ha % 2) * 512 + 512],
                                 s_ksb[:, h:h + 1],
                                 s_qp[:, (ha % 2) * TT:(ha % 2 + 1) * TT],
                                 start=True, stop=True), 1)
                t.wait_ge(zrd.h, ha + 1)
                zbcm.inc(t.matmul(PB[:, (ha % 2) * 512:(ha % 2) * 512 + 512],
                                  s_onr[:, :], s_zb[:, :],
                                  start=True, stop=True), 1)
            t.wait_ge(xns[tt % 2].h, 16 * EC * (tt // 2 + 1))
            t.wait_ge(aev.h, H * (tt + 1))
            for gi in range(NGRP):
                n = tt * NGRP + gi
                t.wait_ge(wga[n % 2].h, WGA_PER * (n // 2 + 1))
                b = (n % 2) * EC * 1024
                for li in range(GS):
                    ic = GS * gi + li
                    icg = tt * NI + ic
                    if icg >= 2:
                        t.wait_ge(sil.h, icg - 1)
                        t.wait_ge(hev.h, icg - 1)
                    for ec in range(EC):
                        t.matmul(PA[:, (ic % 2) * 512:(ic % 2) * 512 + 512],
                                 s_wgu[:, b + 1024 * ec + 128 * li:b + 1024 * ec + 128 * (li + 1)],
                                 s_xn[:, TT * ec:TT * (ec + 1)],
                                 start=(ec == 0), stop=(ec == EC - 1))
                    for ec in range(EC):
                        mm = t.matmul(PA[:, 1024 + (ic % 2) * 512:1024 + (ic % 2) * 512 + 512],
                                      s_wgu[:, b + 1024 * ec + 512 + 128 * li:b + 1024 * ec + 512 + 128 * (li + 1)],
                                      s_xn[:, TT * ec:TT * (ec + 1)],
                                      start=(ec == 0), stop=(ec == EC - 1))
                    gum.inc(mm, 1)
            t.wait_ge(hev.h, NI * (tt + 1))
            t.wait_ge(wos[tt % 2].h, 16 * 64 * (tt // 2 + 1))
            for fc in range(16):
                q = tt * 16 + fc
                t.wait_ge(wdl.h, 16 * (q + 1))
                if q >= 2:
                    t.wait_ge(x2e.h, q - 1)
                for ic in range(NI):
                    t.matmul(PA[:, (q % 2) * 512:(q % 2) * 512 + 512],
                             s_wd[:, 128 * ic:128 * (ic + 1)],
                             s_h[:, TT * ic:TT * (ic + 1)],
                             start=(ic == 0), stop=(ic == NI - 1))
                fglob = 128 * fc
                t.matmul(PA[:, 1024 + (q % 2) * 512:1024 + (q % 2) * 512 + 512],
                         s_obb[0:1, fglob:fglob + 128], s_onr5[:, :],
                         start=True, stop=False)
                for ec in range(EC):
                    mm = t.matmul(PA[:, 1024 + (q % 2) * 512:1024 + (q % 2) * 512 + 512],
                                  s_wgu[:, 2048 * ec + 128 * fc:2048 * ec + 128 * (fc + 1)],
                                  s_attn[:, TT * ec:TT * (ec + 1)],
                                  start=False, stop=(ec == EC - 1))
                fam.inc(mm, 1)

    @block.scalar
    def _(sc):
        for tt in range(NT):
            for h in range(H):
                ha = tt * H + h
                sc.wait_ge(zbcm.h, ha + 1)
                zcp.inc(sc.copy(s_zbc[:, (ha % 2) * 512:(ha % 2) * 512 + 512],
                                PB[:, (ha % 2) * 512:(ha % 2) * 512 + 512]), 1)
            for ic in range(NI):
                icg = tt * NI + ic
                sc.wait_ge(gum.h, icg + 1)
                if icg >= 2:
                    sc.wait_ge(hev.h, icg - 1)
                sil.inc(sc.activation(s_sg[:, (ic % 2) * TT:(ic % 2) * TT + TT],
                                      PA[:, (ic % 2) * 512:(ic % 2) * 512 + 512],
                                      AF.Silu, scale=s_gs[:, ic:ic + 1]), 1)

    @block.vector
    def _(v):
        v.memset(s_onr5[:, :], 1.0)
        onesd.inc(v.memset(s_onr[:, :], 1.0), 1)
        v.wait_ge(misc.h, 16)
        kvr.inc(v.tensor_copy(s_obb[:, 0:1024], s_zbc[0:1, 0:1024]), 1)
        v.wait_ge(misc.h, 32)
        kvr.inc(v.tensor_copy(s_obb[:, 1024:2048], s_zbc[0:1, 0:1024]), 1)
        v.wait_ge(kva.h, 64)
        kvr.inc(v.tensor_copy(s_kvt[:, 0:1024], s_zbc[:, :]), 1)
        v.wait_ge(kva.h, 80)
        v.tensor_copy(s_kvt[:, 1024:2048], s_zbc[:, :])
        kvr.inc(v.tensor_copy(s_ksb[:, :], s_ksf[:, :]), 1)
        for tt in range(NT):
            for h in range(H):
                ha = tt * H + h
                v.wait_ge(amm.h, ha + 1)
                v.tensor_scalar(s_zr[:, :],
                                PC[0:1, (ha % 2) * 512:(ha % 2) * 512 + 512],
                                1e-6, None, ALU.add)
                v.reciprocal(s_zr[:, :], s_zr[:, :])
                if ha >= 1:
                    v.wait_ge(zbcm.h, ha)
                zrd.inc(v.tensor_copy(s_zb[:, :], s_zr[:, :]), 1)
                v.wait_ge(zcp.h, ha + 1)
                aev.inc(v.tensor_mul(s_attn[:, TT * h:TT * (h + 1)],
                                     PA[:, (ha % 2) * 512:(ha % 2) * 512 + 512],
                                     s_zbc[:, (ha % 2) * 512:(ha % 2) * 512 + 512]), 1)
            for ic in range(NI):
                icg = tt * NI + ic
                v.wait_ge(sil.h, icg + 1)
                if tt >= 1 and ic == 0:
                    v.wait_ge(fam.h, 16 * tt)
                hev.inc(v.scalar_tensor_tensor(s_h[:, TT * ic:TT * (ic + 1)],
                                               PA[:, 1024 + (ic % 2) * 512:1024 + (ic % 2) * 512 + 512],
                                               s_us[:, ic:ic + 1],
                                               s_sg[:, (ic % 2) * TT:(ic % 2) * TT + TT],
                                               ALU.mult, ALU.mult), 1)
            for fc in range(16):
                q = tt * 16 + fc
                v.wait_ge(fam.h, q + 1)
                v.wait_ge(xsl[q % 2].h, 16 * (q // 2 + 1))
                if q >= 2:
                    v.wait_ge(x2s[q % 2].h, 16 * (q // 2))
                v.tensor_scalar(s_t1[:, :], PA[:, (q % 2) * 512:(q % 2) * 512 + 512],
                                s_ds[:, fc:fc + 1], None, ALU.mult)
                v.tensor_add(s_t1[:, :], s_t1[:, :],
                             PA[:, 1024 + (q % 2) * 512:1024 + (q % 2) * 512 + 512])
                x2e.inc(v.tensor_add(s_x2[:, (q % 2) * TT:(q % 2 + 1) * TT],
                                     s_t1[:, :], s_xs[:, (q % 2) * TT:(q % 2 + 1) * TT]), 1)

    p.close()
    return p.nc


def host_l1post(inputs, w, l4, l5):
    in_maps = []
    for c in range(NCORE):
        b = c // 2
        ks0 = l5[2 * b]["o_ksp"].reshape(16, 128).T
        ks1 = l5[2 * b + 1]["o_ksp"].reshape(16, 128).T
        in_maps.append(dict(
            x1_T=l4[c]["x1_T"],
            xn1_bf=l5[c]["o_xn1"],
            qp_bf=l5[c]["o_qp"],
            kvp0=l5[2 * b]["o_kvp"], kvp1=l5[2 * b + 1]["o_kvp"],
            ksp0=np.ascontiguousarray(ks0), ksp1=np.ascontiguousarray(ks1),
            l1o_bf=w["l1o"],
            ob_row=np.asarray(inputs["L1_attn_o_b"]).reshape(1, E),
            g1_bf=w["g1"], u1_bf=w["u1"], d1T_bf=w["d1T"],
            gs_pc=np.ascontiguousarray(np.asarray(inputs["L1_ffn_gate_scale"]).reshape(64, 128).T),
            us_pc=np.ascontiguousarray(np.asarray(inputs["L1_ffn_up_scale"]).reshape(64, 128).T),
            ds_pc=np.ascontiguousarray(np.asarray(inputs["L1_ffn_down_scale"]).reshape(16, 128).T),
        ))
    return in_maps


def kernel(**inputs):
    res1 = _run(build_prep(), host_prep(inputs))
    w = assemble_prep(res1.results)
    res2 = _run(build_l0a(), host_l0a(inputs, w))
    l2 = res2.results
    res3 = _run(build_l0attn(), host_l0attn(l2, make_tri()))
    l3 = res3.results
    res4 = _run(build_l0post(), host_l0post(inputs, w, l2, l3))
    l4 = res4.results
    res5 = _run(build_l1a(), host_l1a(inputs, l4, w))
    l5 = res5.results
    res6 = _run(build_l1post(), host_l1post(inputs, w, l4, l5))
    out = np.concatenate([res6.results[c]["xo_T"].T for c in range(NCORE)], axis=0)
    return np.ascontiguousarray(out.reshape(B, S, E)).astype(np.float32)
